# version 01 (final)
# baseline (speedup 1.0000x reference)
import numpy as np
import contextlib
import concourse.bass as bass
import concourse.mybir as mybir
from concourse.bass_utils import run_bass_kernel_spmd

F32 = mybir.dt.float32
BF16 = mybir.dt.bfloat16
I32 = mybir.dt.int32
AF = mybir.ActivationFunctionType
ALU = mybir.AluOpType
AX = mybir.AxisListType.X

NCORES = 8
DM = 4096
DMC = 32
TOK = 1024
NT = 8
SEQ = 2048
CTX = 2048
NKT = 16
H = 32
KVH = 8
HD = 128
SM = float(HD) ** -0.5
EPS = 1e-5
NEG = -1.0e30
RND = 12582912.0
QMAX = 127.0
GRP = 64
WIN = 9


class Sem:
    def __init__(self, handle):
        self.h = handle
        self.v = 0

    def tick(self, instr, n=1):
        instr.then_inc(self.h, n)
        self.v += n
        return self.v


def build_program(dump=None):
    nc = bass.Bass("TRN2", target_bir_lowering=False)

    x_own = nc.dram_tensor("x_own", [TOK, DM], F32, kind="ExternalInput").ap()
    wqkv_t = nc.dram_tensor("wqkv_t", [48, 128, DMC, 128], F32, kind="ExternalInput").ap()
    wo_g = nc.dram_tensor("wo_g", [4, DMC, 128, 1024], F32, kind="ExternalInput").ap()
    wfT_g = nc.dram_tensor("wfT_g", [4, DMC, 128, 1024], F32, kind="ExternalInput").ap()
    wmod_sl = nc.dram_tensor("wmod_sl", [512, 6 * DM], F32, kind="ExternalInput").ap()
    cT_sl = nc.dram_tensor("cT_sl", [512, 4], F32, kind="ExternalInput").ap()
    cosv = nc.dram_tensor("cosv", [128, NT * 64], F32, kind="ExternalInput").ap()
    sinv = nc.dram_tensor("sinv", [128, NT * 64], F32, kind="ExternalInput").ap()
    tri_lo = nc.dram_tensor("tri_lo", [128, 128], F32, kind="ExternalInput").ap()
    tri_hi = nc.dram_tensor("tri_hi", [128, 128], F32, kind="ExternalInput").ap()
    kv01 = nc.dram_tensor("kv01", [128, NKT], F32, kind="ExternalInput").ap()
    identb = nc.dram_tensor("identb", [128, 128], BF16, kind="ExternalInput").ap()
    identf = nc.dram_tensor("identf", [128, 128], F32, kind="ExternalInput").ap()
    onesb = nc.dram_tensor("onesb", [128, 128], BF16, kind="ExternalInput").ap()
    smrow = nc.dram_tensor("smrow", [1, 128], BF16, kind="ExternalInput").ap()
    kvidx = nc.dram_tensor("kvidx", [128, 1], I32, kind="ExternalInput").ap()
    vmask = nc.dram_tensor("vmask", [128, 1], F32, kind="ExternalInput").ap()
    modidx = nc.dram_tensor("modidx", [128, 1], I32, kind="ExternalInput").ap()

    y = nc.dram_tensor("y", [TOK, DM], F32, kind="ExternalOutput").ap()

    ar_in = nc.dram_tensor("ar_in", [4, 6 * DM], F32)
    ar_out = nc.dram_tensor("ar_out", [4, 6 * DM], F32)
    ag_in = nc.dram_tensor("ag_in", [128, 16384], BF16)
    gates_d = nc.dram_tensor("gates_d", [1, 6 * DM], F32)
    gd2 = nc.dram_tensor("gd2", [128, 192], F32)
    ag_out = nc.dram_tensor("ag_out", [128 * NCORES, 16384], BF16)
    attnD = nc.dram_tensor("attnD", [H, 128, NT * 128], BF16,
                           kind="ExternalOutput" if dump == "attn" else "Internal")

    dbg = None
    att0tap = {}
    if dump == "att0":
        att0tap["q"] = nc.dram_tensor("tq5", [128, TOK], F32, kind="ExternalOutput").ap()
        att0tap["pr"] = nc.dram_tensor("tpr", [128, WIN * 128], F32, kind="ExternalOutput").ap()
        att0tap["sm"] = nc.dram_tensor("tsm", [1, 128], F32, kind="ExternalOutput").ap()
    qk0tap = {}
    if dump == "qk0":
        qk0tap["qk"] = nc.dram_tensor("tqk", [128, TOK], F32, kind="ExternalOutput").ap()
        qk0tap["wq"] = nc.dram_tensor("twq", [128, DM], F32, kind="ExternalOutput").ap()
        qk0tap["h1"] = nc.dram_tensor("th1", [128, 2 * TOK], F32, kind="ExternalOutput").ap()
    ropetap = {}
    if dump == "rope0":
        for nm in ("vt0", "tm0", "rt0", "ps0"):
            ropetap[nm] = nc.dram_tensor(nm, [128, TOK], F32, kind="ExternalOutput").ap()
    if dump in ("mod", "h1t", "k", "v", "xq"):
        shp = {
            "mod": [1, 6 * DM],
            "h1t": [128, DMC * TOK],
            "k": [128, KVH * CTX],
            "v": [128, NKT * KVH * HD],
            "xq": [128, DMC * TOK],
        }[dump]
        dbg = nc.dram_tensor("dbg", shp, F32, kind="ExternalOutput").ap()

    ctx = contextlib.ExitStack()
    with ctx:
        qs = Sem(ctx.enter_context(nc.semaphore("qs")))
        qg = Sem(ctx.enter_context(nc.semaphore("qg")))
        qa = Sem(ctx.enter_context(nc.semaphore("qa")))
        pe = Sem(ctx.enter_context(nc.semaphore("pe")))
        dv = Sem(ctx.enter_context(nc.semaphore("dv")))
        ac = Sem(ctx.enter_context(nc.semaphore("ac")))
        po = Sem(ctx.enter_context(nc.semaphore("po")))
        cc = Sem(ctx.enter_context(nc.semaphore("cc")))

        block = ctx.enter_context(nc.Block())

        sb = lambda name, shape, dt: ctx.enter_context(nc.sbuf_tensor(name, shape, dt))
        identb_sb = sb("identb_sb", [128, 128], BF16)
        identf_sb = sb("identf_sb", [128, 128], F32)
        onesb_sb = sb("onesb_sb", [128, 128], BF16)
        smrow_sb = sb("smrow_sb", [1, 128], BF16)
        tri_lo_sb = sb("tri_lo_sb", [128, 128], F32)
        tri_hi_sb = sb("tri_hi_sb", [128, 128], F32)
        kv01_sb = sb("kv01_sb", [128, NKT], BF16)
        kvidx_sb = sb("kvidx_sb", [128, 1], I32)
        vmask_sb = sb("vmask_sb", [128, 1], F32)
        modidx_sb = sb("modidx_sb", [128, 1], I32)

        @block.sync
        def _(e):
            for dst, src in ((identb_sb, identb), (identf_sb, identf),
                             (onesb_sb, onesb), (smrow_sb, smrow),
                             (tri_lo_sb, tri_lo), (tri_hi_sb, tri_hi),
                             (kvidx_sb, kvidx), (modidx_sb, modidx)):
                qs.tick(e.dma_start(out=dst[:], in_=src), 16)

        @block.gpsimd
        def _(e):
            qg.tick(e.dma_start(out=kv01_sb[:], in_=kv01), 16)
            qg.tick(e.dma_start(out=vmask_sb[:], in_=vmask), 16)

        with contextlib.ExitStack() as _S:
            cs_sb = _S.enter_context(nc.sbuf_tensor("cs_sb", [128, 16], F32))
            csb_sb = _S.enter_context(nc.sbuf_tensor("csb_sb", [128, 16], BF16))
            wm_sb = _S.enter_context(nc.sbuf_tensor("wm_sb", [128, 2 * 8192], BF16))
            ctmp = _S.enter_context(nc.sbuf_tensor("ctmp", [4, 2048], F32))
            modsel = _S.enter_context(nc.sbuf_tensor("modsel", [1, 6 * DM], F32))
            mtmp = _S.enter_context(nc.sbuf_tensor("mtmp", [128, 192], F32))
            scr = _S.enter_context(nc.sbuf_tensor("scr", [1, DM], F32))
            md_ps = _S.enter_context(nc.psum_tensor("md_ps", [4, 2048], F32))
            NCH = 12

            @block.sync
            def _(e):
                qs.tick(e.dma_start(
                    out=cs_sb[:].rearrange("p (kc m) -> p kc m", m=4),
                    in_=cT_sl.rearrange("(kc p) m -> p kc m", p=128)), 16)
            cs_ld = qs.v

            @block.scalar
            def _(e):
                e.wait_ge(qs.h, cs_ld)
                ac.tick(e.activation(csb_sb[:], cs_sb[:], AF.Silu))
            silu_done = ac.v

            wm_ld = [None] * NCH
            mm_done = [None] * NCH
            ev_done = [None] * NCH
            st_done = [None] * NCH

            def emit_wm_load(ch, wait_pe=None):
                @block.gpsimd
                def _(e):
                    if wait_pe is not None:
                        e.wait_ge(pe.h, wait_pe)
                    qg.tick(e.dma_start(
                        out=wm_sb[:, (ch % 2) * 8192:(ch % 2 + 1) * 8192]
                        .rearrange("p (kc n) -> p kc n", n=2048),
                        in_=wmod_sl[:, ch * 2048:(ch + 1) * 2048]
                        .rearrange("(kc p) n -> p kc n", p=128)), 16)
                wm_ld[ch] = qg.v

            emit_wm_load(0)
            emit_wm_load(1)
            for ch in range(NCH):
                @block.tensor
                def _(e, ch=ch):
                    if ch == 0:
                        e.wait_ge(ac.h, silu_done)
                    e.wait_ge(qg.h, wm_ld[ch])
                    if ch >= 1:
                        e.wait_ge(dv.h, ev_done[ch - 1])
                    for quarter in range(4):
                        for kc in range(4):
                            pe.tick(e.matmul(
                                md_ps[:, quarter * 512:(quarter + 1) * 512],
                                csb_sb[:, kc * 4:(kc + 1) * 4],
                                wm_sb[:, (ch % 2) * 8192 + kc * 2048 + quarter * 512:
                                      (ch % 2) * 8192 + kc * 2048 + (quarter + 1) * 512],
                                start=(kc == 0), stop=(kc == 3)))
                mm_done[ch] = pe.v

                @block.vector
                def _(e, ch=ch):
                    e.wait_ge(pe.h, mm_done[ch])
                    if ch >= 1:
                        e.wait_ge(qs.h, st_done[ch - 1])
                    dv.tick(e.tensor_copy(ctmp[:], md_ps[:]))
                ev_done[ch] = dv.v

                @block.sync
                def _(e, ch=ch):
                    e.wait_ge(dv.h, ev_done[ch])
                    qs.tick(e.dma_start(out=ar_in.ap()[:, ch * 2048:(ch + 1) * 2048],
                                        in_=ctmp[:]), 16)
                st_done[ch] = qs.v

                if ch + 2 < NCH:
                    emit_wm_load(ch + 2, wait_pe=mm_done[ch])

            contrib_done = qs.v

            @block.gpsimd
            def _(e):
                e.wait_ge(qs.h, contrib_done)
                cc.tick(e.collective_compute(
                    "AllReduce", ALU.add,
                    replica_groups=[list(range(NCORES))],
                    ins=[ar_in.ap().rearrange("a (b c) -> (a b) c", b=32).opt()],
                    outs=[ar_out.ap().rearrange("a (b c) -> (a b) c", b=32).opt()]))
                e.wait_ge(cc.h, cc.v)
                qg.tick(e.indirect_dma_start(
                    out=mtmp[:], out_offset=None,
                    in_=ar_out.ap().rearrange("a (p f) -> (a p) f", f=192),
                    in_offset=bass.IndirectOffsetOnAxis(modidx_sb[:], 0)), 16)
                e.wait_ge(qg.h, qg.v)
                qg.tick(e.dma_start(out=gd2.ap(), in_=mtmp[:]), 16)
                e.wait_ge(qg.h, qg.v)
                qg.tick(e.dma_start(out=modsel[:],
                                    in_=gd2.ap().rearrange("p f -> (p f)").unsqueeze(0)), 16)
            modsel_ld = qg.v

            @block.vector
            def _(e):
                e.wait_ge(qg.h, modsel_ld)
                SA, CA, GA, SBo, CB, GB = (i * DM for i in range(6))
                r = lambda off: modsel[0:1, off:off + DM]
                dv.tick(e.tensor_scalar(r(CA), r(CA), 1.0, None, ALU.add))
                dv.tick(e.tensor_scalar(r(CB), r(CB), 1.0, None, ALU.add))
                for off in (GA, GB):
                    dv.tick(e.tensor_scalar(scr[0:1, :], r(off), 0.0, None, ALU.is_ge))
                    e.wait_ge(dv.h, dv.v)
                    dv.tick(e.tensor_scalar(scr[0:1, :], scr[0:1, :], 2.0, -1.0,
                                            ALU.mult, ALU.add))
                    dv.tick(e.scalar_tensor_tensor(r(off), r(off), -1.0, r(off),
                                                   ALU.mult, ALU.max))
                    e.wait_ge(dv.h, dv.v)
                    dv.tick(e.tensor_scalar(r(off), r(off), 0.25, None, ALU.max))
                    e.wait_ge(dv.h, dv.v)
                    dv.tick(e.tensor_tensor(r(off), r(off), scr[0:1, :], ALU.mult))
                    e.wait_ge(dv.h, dv.v)
            gates_prep = dv.v

            @block.sync
            def _(e):
                e.wait_ge(dv.h, gates_prep)
                qs.tick(e.dma_start(out=gates_d.ap(), in_=modsel[:]), 16)
            gates_done = qs.v

        if dump == "mod":
            @block.gpsimd
            def _(e):
                e.wait_ge(qs.h, gates_done)
                qg.tick(e.dma_start(out=dbg, in_=gates_d.ap()), 16)
                e.wait_ge(qg.h, qg.v)
            return nc

        def emit_bcast(off, out_sb, rowbf_sb, bc_ps, extra_wait=None):
            @block.gpsimd
            def _(e):
                e.wait_ge(qs.h, gates_done)
                if extra_wait is not None:
                    s, v = extra_wait
                    e.wait_ge(s.h, v)
                e.wait_ge(ac.h, ac.v)
                qg.tick(e.dma_start(out=rowbf_sb[0:1, :],
                                    in_=gates_d.ap()[0:1, off:off + DM]), 16)
            rdone = qg.v
            last = None
            for q in range(4):
                @block.tensor
                def _(e, q=q):
                    e.wait_ge(qg.h, rdone)
                    e.wait_ge(ac.h, ac.v)
                    for hf in range(2):
                        pe.tick(e.matmul(bc_ps[:, hf * 512:(hf + 1) * 512],
                                         onesb_sb[0:1, :],
                                         rowbf_sb[0:1, q * 1024 + hf * 512:
                                                  q * 1024 + (hf + 1) * 512],
                                         start=True, stop=True))
                bdone = pe.v

                @block.scalar
                def _(e, q=q, bdone=bdone):
                    e.wait_ge(pe.h, bdone)
                    ac.tick(e.activation(out_sb[:, q * 1024:(q + 1) * 1024], bc_ps[:],
                                         AF.Copy, bias=0.0))
                last = ac.v
            return last

        with contextlib.ExitStack() as _S:
            h1T = _S.enter_context(nc.sbuf_tensor("h1T", [128, DMC * TOK], BF16))
            K_sb = _S.enter_context(nc.sbuf_tensor("K_sb", [128, KVH * CTX], BF16))
            V_sb = _S.enter_context(nc.sbuf_tensor("V_sb", [128, NKT * KVH * HD], BF16))
            @block.gpsimd
            def _(e):
                e.wait_ge(qs.h, gates_done)
                po.tick(e.memset(K_sb[:], 0.0))
                po.tick(e.memset(V_sb[:], 0.0))
            kv_zero = po.v

            with contextlib.ExitStack() as _S:
                xbuf = _S.enter_context(nc.sbuf_tensor("xbuf", [128, 3 * 2048], F32))
                sqd = _S.enter_context(nc.sbuf_tensor("sqd", [128, 2048], F32))
                h1tm = _S.enter_context(nc.sbuf_tensor("h1tm", [128, DM], BF16))
                caB = _S.enter_context(nc.sbuf_tensor("caB", [128, DM], BF16))
                saB = _S.enter_context(nc.sbuf_tensor("saB", [128, DM], BF16))
                rowbf = _S.enter_context(nc.sbuf_tensor("rowbf", [1, DM], BF16))
                stat = _S.enter_context(nc.sbuf_tensor("stat", [128, 16], F32))
                bc_ps = _S.enter_context(nc.psum_tensor("bc_ps", [128, 1024], F32))
                tp_ps = _S.enter_context(nc.psum_tensor("tp_ps", [128, 2048], BF16))
                caB_done = emit_bcast(1 * DM, caB, rowbf, bc_ps)
                saB_done = emit_bcast(0 * DM, saB, rowbf, bc_ps)

                hx_ld = [None] * (2 * NT)
                hx_free = [None] * (2 * NT)
                sq_done = [None] * (2 * NT)
                h1_done = [None] * NT
                tp_done = [None] * NT
                h1t_ev = [None] * NT

                def emit_half_load(hi):
                    t, a = divmod(hi, 2)

                    @block.sync
                    def _(e):
                        if hi == 0:
                            e.wait_ge(qs.h, gates_done)
                        if hi >= 3:
                            e.wait_ge(dv.h, hx_free[hi - 3])
                            e.wait_ge(ac.h, sq_done[hi - 3])
                        qs.tick(e.dma_start(
                            out=xbuf[:, (hi % 3) * 2048:(hi % 3 + 1) * 2048],
                            in_=x_own[t * 128:(t + 1) * 128, a * 2048:(a + 1) * 2048]), 16)
                    hx_ld[hi] = qs.v

                for hi in range(3):
                    emit_half_load(hi)

                for t in range(NT):
                    for a in range(2):
                        hi = 2 * t + a
                        xs = xbuf[:, (hi % 3) * 2048:(hi % 3 + 1) * 2048]

                        @block.scalar
                        def _(e, hi=hi, xs=xs, a=a):
                            e.wait_ge(qs.h, hx_ld[hi])
                            if hi >= 2:
                                e.wait_ge(dv.h, dv.v)
                            ac.tick(e.activation(sqd[:], xs, AF.Square,
                                                 accum_out=stat[:, 2 + a:3 + a]))
                        sq_done[hi] = ac.v

                        @block.vector
                        def _(e, hi=hi, xs=xs, a=a):
                            e.wait_ge(qs.h, hx_ld[hi])
                            dv.tick(e.tensor_reduce(stat[:, 0 + a:1 + a], xs,
                                                    axis=AX, op=ALU.add))

                    @block.vector
                    def _(e, t=t):
                        e.wait_ge(ac.h, sq_done[2 * t + 1])
                        e.wait_ge(dv.h, dv.v)
                        dv.tick(e.tensor_tensor(stat[:, 4:5], stat[:, 0:1], stat[:, 1:2], ALU.add))
                        dv.tick(e.tensor_tensor(stat[:, 5:6], stat[:, 2:3], stat[:, 3:4], ALU.add))
                        e.wait_ge(dv.h, dv.v)
                        dv.tick(e.tensor_scalar(stat[:, 6:7], stat[:, 4:5], 1.0 / DM, None, ALU.mult))
                        dv.tick(e.tensor_scalar(stat[:, 7:8], stat[:, 5:6], 1.0 / DM, None, ALU.mult))
                        e.wait_ge(dv.h, dv.v)
                        dv.tick(e.tensor_tensor(stat[:, 8:9], stat[:, 6:7], stat[:, 6:7], ALU.mult))
                        e.wait_ge(dv.h, dv.v)
                        dv.tick(e.tensor_tensor(stat[:, 9:10], stat[:, 7:8], stat[:, 8:9], ALU.subtract))
                        e.wait_ge(dv.h, dv.v)
                        dv.tick(e.tensor_scalar(stat[:, 9:10], stat[:, 9:10], EPS, None, ALU.add))
                    var_done = dv.v

                    @block.scalar
                    def _(e, t=t, var_done=var_done):
                        e.wait_ge(dv.h, var_done)
                        e.activation(stat[:, 10:11], stat[:, 9:10], AF.Ln)
                        ac.tick(e.activation(stat[:, 11:12], stat[:, 10:11], AF.Exp, scale=-0.5))
                    rstd_done = ac.v

                    @block.vector
                    def _(e, t=t, rstd_done=rstd_done):
                        e.wait_ge(ac.h, rstd_done)
                        dv.tick(e.tensor_tensor(stat[:, 12:13], stat[:, 6:7], stat[:, 11:12], ALU.mult))
                        e.wait_ge(dv.h, dv.v)
                        dv.tick(e.tensor_scalar(stat[:, 13:14], stat[:, 12:13], -1.0, None, ALU.mult))
                        e.wait_ge(dv.h, dv.v)
                        if t == 0:
                            e.wait_ge(ac.h, max(caB_done, saB_done))
                        if t >= 1:
                            e.wait_ge(pe.h, tp_done[t - 1])
                        for a in range(2):
                            hi = 2 * t + a
                            xs = xbuf[:, (hi % 3) * 2048:(hi % 3 + 1) * 2048]
                            hsl = slice(a * 2048, (a + 1) * 2048)
                            dv.tick(e.tensor_scalar(sqd[:], xs, stat[:, 11:12],
                                                    stat[:, 13:14], ALU.mult, ALU.add))
                            hx_free[hi] = dv.v
                            e.wait_ge(dv.h, dv.v)
                            dv.tick(e.tensor_tensor(sqd[:], sqd[:], caB[:, hsl], ALU.mult))
                            e.wait_ge(dv.h, dv.v)
                            dv.tick(e.tensor_tensor(h1tm[:, hsl], sqd[:], saB[:, hsl], ALU.add))
                            e.wait_ge(dv.h, dv.v)
                    h1_done[t] = dv.v

                    for grp16 in range(2):
                        @block.tensor
                        def _(e, t=t, grp16=grp16):
                            e.wait_ge(dv.h, h1_done[t])
                            e.wait_ge(ac.h, ac.v)
                            for k in range(16):
                                kc = grp16 * 16 + k
                                pe.tick(e.transpose(tp_ps[:, k * 128:(k + 1) * 128],
                                                    h1tm[:, kc * 128:(kc + 1) * 128],
                                                    identb_sb[:]))
                        tp_done[t] = pe.v

                        @block.scalar
                        def _(e, t=t, grp16=grp16):
                            e.wait_ge(pe.h, pe.v)
                            ac.tick(e.activation(
                                h1T[:, :].rearrange("p (kc tt) -> p kc tt", tt=TOK)
                                [:, grp16 * 16:(grp16 + 1) * 16, t * 128:(t + 1) * 128],
                                tp_ps[:].rearrange("p (k q) -> p k q", q=128),
                                AF.Copy, bias=0.0))
                        h1t_ev[t] = ac.v

                    for aa in range(2):
                        hi = 2 * t + aa + 3
                        if hi < 2 * NT:
                            emit_half_load(hi)

                h1_all = ac.v

            if dump == "h1t":
                @block.gpsimd
                def _(e):
                    e.wait_ge(ac.h, h1_all)
                    qg.tick(e.dma_start(out=dbg, in_=h1T[:]), 16)
                    e.wait_ge(qg.h, qg.v)
                return nc

            with contextlib.ExitStack() as _S:
                wq_sb = _S.enter_context(nc.sbuf_tensor("wq_sb", [128, 2 * DM], BF16))
                ro_sb = _S.enter_context(nc.sbuf_tensor("ro_sb", [128, TOK], F32))
                vtmp = _S.enter_context(nc.sbuf_tensor("vtmp", [128, TOK], BF16))
                ttm = _S.enter_context(nc.sbuf_tensor("ttm", [128, TOK], F32))
                kvstage = _S.enter_context(nc.sbuf_tensor("kvstage", [128, KVH * TOK], BF16))
                rtm = _S.enter_context(nc.sbuf_tensor("rtm", [128, TOK], BF16))
                cos_sb = _S.enter_context(nc.sbuf_tensor("cos_sb", [128, NT * 64], F32))
                sin_sb = _S.enter_context(nc.sbuf_tensor("sin_sb", [128, NT * 64], F32))
                _SP2 = contextlib.ExitStack()
                tp2_ps = _SP2.enter_context(nc.psum_tensor("tp2_ps", [128, 1024], BF16))
                qk_ps = _SP2.enter_context(nc.psum_tensor("qk_ps", [128, 2 * TOK], F32))
                @block.sync
                def _(e):
                    e.wait_ge(ac.h, h1_all)
                    qs.tick(e.dma_start(out=cos_sb[:], in_=cosv), 16)
                    qs.tick(e.dma_start(out=sin_sb[:], in_=sinv), 16)
                rope_tab = qs.v

                w_ld = {}
                kv_mm = {}
                kv_free = {}

                def emit_w_load(qc, wait_pe=None, slot=None):
                    @block.gpsimd
                    def _(e):
                        if wait_pe is not None:
                            e.wait_ge(pe.h, wait_pe)
                        qg.tick(e.dma_start(
                            out=wq_sb[:, slot * DM:(slot + 1) * DM],
                            in_=wqkv_t[qc].rearrange("p kc c -> p (kc c)")), 16)
                    w_ld[qc] = qg.v

                @block.gpsimd
                def _(e):
                    e.wait_ge(ac.h, h1_all)
                emit_w_load(32, slot=0)
                emit_w_load(33, slot=1)

                tp2_state = {"evac": 0}

                def emit_rope(P, dst, base, mm_wait, tp_ps, tap=False):
                    @block.scalar
                    def _(e):
                        e.wait_ge(pe.h, mm_wait)
                        e.wait_ge(dv.h, dv.v)
                        ac.tick(e.activation(vtmp[:], P, AF.Copy, bias=0.0))
                    p_free = ac.v
                    if tap:
                        @block.gpsimd
                        def _(e):
                            e.wait_ge(ac.h, p_free)
                            qg.tick(e.dma_start(out=ropetap["vt0"], in_=vtmp[:]), 16)
                        tapv = qg.v
                        @block.tensor
                        def _(e, tapv=tapv):
                            e.wait_ge(qg.h, tapv)

                    @block.tensor
                    def _(e):
                        e.wait_ge(ac.h, p_free)
                        e.wait_ge(ac.h, tp2_state["evac"])
                        for kt in range(NT):
                            pe.tick(e.transpose(tp_ps[:, kt * 128:(kt + 1) * 128],
                                                vtmp[:, kt * 128:(kt + 1) * 128],
                                                identb_sb[:]))
                    tr1 = pe.v

                    @block.scalar
                    def _(e):
                        e.wait_ge(pe.h, tr1)
                        ac.tick(e.activation(ttm[:], tp_ps[:, 0:TOK], AF.Copy, bias=0.0))
                    tp2_state["evac"] = ac.v
                    ttm_done = ac.v
                    if tap:
                        @block.gpsimd
                        def _(e):
                            e.wait_ge(ac.h, ttm_done)
                            qg.tick(e.dma_start(out=ropetap["tm0"], in_=ttm[:]), 16)
                        tapt = qg.v
                        @block.vector
                        def _(e, tapt=tapt):
                            e.wait_ge(qg.h, tapt)

                    @block.vector
                    def _(e):
                        e.wait_ge(ac.h, ttm_done)
                        v3 = lambda x, a: x.rearrange("p (kt d) -> p kt d", d=128)[:, :, a * 64:(a + 1) * 64]
                        c3 = lambda x: x.rearrange("p (kt d) -> p kt d", d=64)
                        A = ro_sb[:, 0:512]
                        B = ro_sb[:, 512:1024]
                        dv.tick(e.tensor_tensor(c3(A), v3(ttm, 0), c3(cos_sb[:]), ALU.mult))
                        dv.tick(e.tensor_tensor(c3(B), v3(ttm, 1), c3(sin_sb[:]), ALU.mult))
                        e.wait_ge(dv.h, dv.v)
                        dv.tick(e.tensor_tensor(v3(rtm, 0), c3(A), c3(B), ALU.subtract))
                        e.wait_ge(dv.h, dv.v)
                        dv.tick(e.tensor_tensor(c3(A), v3(ttm, 0), c3(sin_sb[:]), ALU.mult))
                        dv.tick(e.tensor_tensor(c3(B), v3(ttm, 1), c3(cos_sb[:]), ALU.mult))
                        e.wait_ge(dv.h, dv.v)
                        dv.tick(e.tensor_tensor(v3(rtm, 1), c3(A), c3(B), ALU.add))
                    rope_dv = dv.v
                    if tap:
                        @block.gpsimd
                        def _(e):
                            e.wait_ge(dv.h, rope_dv)
                            qg.tick(e.dma_start(out=ropetap["rt0"], in_=rtm[:]), 16)
                        tapr = qg.v
                        @block.tensor
                        def _(e, tapr=tapr):
                            e.wait_ge(qg.h, tapr)

                    @block.tensor
                    def _(e):
                        e.wait_ge(dv.h, rope_dv)
                        e.wait_ge(ac.h, tp2_state["evac"])
                        for kt in range(NT):
                            pe.tick(e.transpose(tp_ps[:, kt * 128:(kt + 1) * 128],
                                                rtm[:, kt * 128:(kt + 1) * 128],
                                                identb_sb[:]))
                    tr2 = pe.v

                    @block.scalar
                    def _(e):
                        e.wait_ge(pe.h, tr2)
                        ac.tick(e.activation(dst[:, base:base + TOK],
                                             tp_ps[:, 0:TOK], AF.Copy, bias=0.0))
                    tp2_state["evac"] = ac.v
                    return p_free, ac.v

                k_rope_ac = []
                for i, qc in enumerate(range(32, 48)):
                    pp = i % 2

                    @block.tensor
                    def _(e, qc=qc, i=i, pp=pp):
                        e.wait_ge(qg.h, w_ld[qc])
                        if i == 0:
                            e.wait_ge(ac.h, h1_all)
                        if i >= 2:
                            s, v = kv_free[qc - 2]
                            e.wait_ge(s.h, v)
                        for hf in range(2):
                            for kc in range(DMC):
                                pe.tick(e.matmul(
                                    qk_ps[:, pp * TOK + hf * 512:pp * TOK + (hf + 1) * 512],
                                    wq_sb[:, pp * DM + kc * 128:pp * DM + (kc + 1) * 128],
                                    h1T[:, kc * TOK + hf * 512:kc * TOK + (hf + 1) * 512],
                                    start=(kc == 0), stop=(kc == DMC - 1)))
                    kv_mm[qc] = pe.v

                    P = qk_ps[:, pp * TOK:(pp + 1) * TOK]

                    if dump == "qk0" and qc == 32:
                        @block.vector
                        def _(e):
                            e.wait_ge(pe.h, kv_mm[32])
                            dv.tick(e.tensor_copy(ttm[:], qk_ps[:, 0:TOK]))
                        qk0_dv = dv.v

                        @block.gpsimd
                        def _(e, qk0_dv=qk0_dv):
                            e.wait_ge(dv.h, qk0_dv)
                            qg.tick(e.dma_start(out=qk0tap["qk"], in_=ttm[:]), 16)
                            qg.tick(e.dma_start(out=qk0tap["wq"], in_=wq_sb[:, 0:DM]), 16)
                            qg.tick(e.dma_start(out=qk0tap["h1"], in_=h1T[:, 0:2 * TOK]), 16)
                            e.wait_ge(qg.h, qg.v)
                        return nc

                    if qc < 40:
                        hh = qc - 32

                        @block.scalar
                        def _(e, qc=qc):
                            if qc == 32:
                                e.wait_ge(qs.h, rope_tab)
                                e.wait_ge(po.h, kv_zero)
                        p_free, k_done = emit_rope(P, K_sb[:, :], hh * CTX + TOK,
                                                   kv_mm[qc], tp2_ps,
                                                   tap=(dump == "rope0" and qc == 32))
                        kv_free[qc] = (ac, p_free)
                        k_rope_ac.append(k_done)
                    else:
                        hh = qc - 40

                        @block.scalar
                        def _(e, qc=qc, P=P):
                            e.wait_ge(pe.h, kv_mm[qc])
                            e.wait_ge(po.h, po.v)
                            e.wait_ge(pe.h, pe.v)
                            ac.tick(e.activation(vtmp[:], P, AF.Copy, bias=0.0))
                        vev = ac.v
                        kv_free[qc] = (ac, vev)

                        @block.tensor
                        def _(e, vev=vev):
                            e.wait_ge(ac.h, vev)
                            e.wait_ge(ac.h, tp2_state["evac"])
                            e.wait_ge(po.h, po.v)
                            for kt in range(NT):
                                pe.tick(e.transpose(tp2_ps[:, kt * 128:(kt + 1) * 128],
                                                    vtmp[:, kt * 128:(kt + 1) * 128],
                                                    identb_sb[:]))
                        vt_mm = pe.v

                        @block.vector
                        def _(e, hh=hh, vt_mm=vt_mm):
                            e.wait_ge(pe.h, vt_mm)
                            if qc == 40:
                                e.wait_ge(po.h, kv_zero)
                            po.tick(e.tensor_copy(
                                V_sb[:, :].rearrange("p (kt g d) -> p kt g d", g=KVH, d=HD)
                                [:, NT:NKT, hh, :],
                                tp2_ps[:, 0:TOK].rearrange("p (kt d) -> p kt d", d=HD)))

                    if qc + 2 < 48:
                        emit_w_load(qc + 2, wait_pe=kv_mm[qc], slot=i % 2)

                kv_ready_ac = max(k_rope_ac)
                kv_ready_po = po.v

                import os as _os
                _skip_ex = _os.environ.get("SKIP_EXCHANGE") == "1"
                @block.sync
                def _(e):
                    e.wait_ge(ac.h, kv_ready_ac)
                    e.wait_ge(po.h, kv_ready_po)
                    qs.tick(e.dma_start(
                        out=ag_in.ap()[:, 0:KVH * TOK].rearrange("p (g t) -> p g t", t=TOK),
                        in_=K_sb[:, :].rearrange("p (g t) -> p g t", t=CTX)
                        [:, :, TOK:CTX]), 16)
                    qs.tick(e.dma_start(out=ag_in.ap()[:, KVH * TOK:16384],
                                        in_=V_sb[:, NT * KVH * HD:]), 16)
                ag_staged = qs.v

                @block.gpsimd
                def _(e):
                    e.wait_ge(qs.h, ag_staged)
                    if _skip_ex:
                        return
                    cc.tick(e.collective_compute(
                        "AllGather", ALU.bypass,
                        replica_groups=[list(range(NCORES))],
                        ins=[ag_in.ap().opt()],
                        outs=[ag_out.ap().opt()]))
                    e.wait_ge(cc.h, cc.v)
                    qg.tick(e.indirect_dma_start(
                        out=kvstage[:],
                        out_offset=None,
                        in_=ag_out.ap(),
                        in_offset=bass.IndirectOffsetOnAxis(kvidx_sb[:], 0),
                        bounds_check=128 * NCORES - 1, oob_is_err=False), 16)
                    e.wait_ge(qg.h, qg.v)
                    qg.tick(e.dma_start(
                        out=K_sb[:, :].rearrange("p (g t) -> p g t", t=CTX)[:, :, 0:TOK],
                        in_=kvstage[:].rearrange("p (g t) -> p g t", t=TOK)), 16)
                    qg.tick(e.indirect_dma_start(
                        out=V_sb[:, 0:NT * KVH * HD],
                        out_offset=None,
                        in_=ag_out.ap(),
                        in_offset=bass.IndirectOffsetOnAxis(kvidx_sb[:], 0),
                        element_offset=KVH * TOK,
                        bounds_check=128 * NCORES - 1, oob_is_err=False), 16)
                kv_gath = qg.v

                @block.vector
                def _(e):
                    e.wait_ge(qg.h, kv_gath)
                    dv.tick(e.tensor_scalar(V_sb[:, 0:NT * KVH * HD],
                                            V_sb[:, 0:NT * KVH * HD],
                                            vmask_sb[:, 0:1], None, ALU.mult))
                vmask_dv = dv.v
                kv_done = qg.v

                if dump == "rope0":
                    @block.gpsimd
                    def _(e):
                        e.wait_ge(ac.h, kv_ready_ac)
                        e.wait_ge(qg.h, qg.v)
                    return nc

                if dump in ("k", "v"):
                    @block.gpsimd
                    def _(e):
                        e.wait_ge(qg.h, kv_done)
                        e.wait_ge(dv.h, vmask_dv)
                        e.wait_ge(ac.h, kv_ready_ac)
                        e.wait_ge(po.h, kv_ready_po)
                        src = {"k": K_sb, "v": V_sb}[dump]
                        qg.tick(e.dma_start(out=dbg, in_=src[:]), 16)
                        e.wait_ge(qg.h, qg.v)
                    return nc

                _SP2.close()
                with contextlib.ExitStack() as _S:
                    Qh = _S.enter_context(nc.sbuf_tensor("Qh", [128, 3 * TOK], BF16))
                    probsT = _S.enter_context(nc.sbuf_tensor("probsT", [128, WIN * 128], BF16))
                    m1 = _S.enter_context(nc.sbuf_tensor("m1", [128, 1], F32))
                    mneg = _S.enter_context(nc.sbuf_tensor("mneg", [1, 1], BF16))
                    bias_sb = _S.enter_context(nc.sbuf_tensor("bias_sb", [128, 1], F32))
                    rsum = _S.enter_context(nc.sbuf_tensor("rsum", [1, 128], BF16))
                    rb_sb = _S.enter_context(nc.sbuf_tensor("rb_sb", [128, 128], BF16))
                    attnU = _S.enter_context(nc.sbuf_tensor("attnU", [128, 2 * 128], BF16))
                    attnH = _S.enter_context(nc.sbuf_tensor("attnH", [128, 2 * NT * 128], BF16))
                    sc_ps0 = _S.enter_context(nc.psum_tensor("sc_ps0", [128, WIN * 128 + 128], F32))
                    sc_ps1 = _S.enter_context(nc.psum_tensor("sc_ps1", [128, WIN * 128 + 128], F32))
                    aux_ps = _S.enter_context(nc.psum_tensor("aux_ps", [128, 512], F32))
                    tp5_ps = _S.enter_context(nc.psum_tensor("tp5_ps", [128, 1024], BF16))
                    sc_ps = [sc_ps0, sc_ps1]
                    slot_free = {-2: (qg, kv_done), -1: (qg, kv_done)}
                    uc = 0
                    pv_evac = {-2: ac.v, -1: ac.v}
                    pv_free = {-2: dv.v, -1: dv.v}
                    exp_of = {}
                    attnH_st = {-2: (qa, qa.v), -1: (qa, qa.v)}

                    for h in range(H):
                        kvh = h // 4
                        qsl = (h % 3)
                        @block.gpsimd
                        def _(e, h=h):
                            s, v = slot_free.get(uc - 2, (qg, kv_done))
                            if h >= 2:
                                e.wait_ge(pe.h, qproj_mm[h - 2])
                            qg.tick(e.dma_start(
                                out=wq_sb[:, (h % 2) * DM:(h % 2 + 1) * DM],
                                in_=wqkv_t[h].rearrange("p kc c -> p (kc c)")), 16)
                        qw_ld = qg.v

                        P = uc % 2

                        @block.tensor
                        def _(e, h=h, P=P, qw_ld=qw_ld):
                            e.wait_ge(qg.h, qw_ld)
                            if h == 0:
                                e.wait_ge(qg.h, kv_done)
                                e.wait_ge(dv.h, vmask_dv)
                            s, v = slot_free[uc - 2]
                            e.wait_ge(s.h, v)
                            for hf in range(2):
                                for kc in range(DMC):
                                    pe.tick(e.matmul(
                                        sc_ps[P][:, hf * 512:(hf + 1) * 512],
                                        wq_sb[:, (h % 2) * DM + kc * 128:(h % 2) * DM + (kc + 1) * 128],
                                        h1T[:, kc * TOK + hf * 512:kc * TOK + (hf + 1) * 512],
                                        start=(kc == 0), stop=(kc == DMC - 1)))
                        if h == 0:
                            qproj_mm = {}
                        qproj_mm[h] = pe.v

                        p_free_q, q_done = emit_rope(sc_ps[P][:, 0:TOK], Qh[:, :],
                                                     qsl * TOK, qproj_mm[h], tp5_ps)
                        slot_free[uc] = (ac, p_free_q)
                        pv_free[uc] = dv.v
                        pv_evac[uc] = ac.v
                        qrope_ac = q_done
                        uc += 1

                        for t in range(NT):
                            P = uc % 2
                            Qs = Qh[:, qsl * TOK + t * 128:qsl * TOK + (t + 1) * 128]

                            @block.tensor
                            def _(e, t=t, P=P, kvh=kvh, Qs=Qs, uc=uc, qrope_ac=qrope_ac):
                                if t == 0:
                                    e.wait_ge(ac.h, qrope_ac)
                                s, v = slot_free[uc - 2]
                                e.wait_ge(s.h, v)
                                for j in range(WIN):
                                    kt = t + j
                                    pe.tick(e.matmul(
                                        sc_ps[P][:, j * 128:(j + 1) * 128],
                                        K_sb[:, kvh * CTX + kt * 128:kvh * CTX + (kt + 1) * 128],
                                        Qs, start=True, stop=True))
                            sc_ready = pe.v

                            @block.vector
                            def _(e, P=P, sc_ready=sc_ready):
                                e.wait_ge(pe.h, sc_ready)
                                dv.tick(e.tensor_tensor(sc_ps[P][:, 0:128], sc_ps[P][:, 0:128],
                                                        tri_lo_sb[:], ALU.add))
                                dv.tick(e.tensor_tensor(sc_ps[P][:, 8 * 128:WIN * 128],
                                                        sc_ps[P][:, 8 * 128:WIN * 128],
                                                        tri_hi_sb[:], ALU.add))
                                e.wait_ge(dv.h, dv.v)
                                dv.tick(e.tensor_reduce(m1[:], sc_ps[P][:, 0:WIN * 128],
                                                        axis=AX, op=ALU.max, negate=True))
                            msk_done = dv.v

                            @block.tensor
                            def _(e, msk_done=msk_done):
                                e.wait_ge(dv.h, msk_done)
                                pe.tick(e.transpose(aux_ps[0:1, 384:512], m1[:], identf_sb[:]))
                            tr_done = pe.v

                            @block.vector
                            def _(e, tr_done=tr_done):
                                e.wait_ge(pe.h, tr_done)
                                with nc.allow_low_precision(
                                        reason="min-reduce to bf16; softmax shift is "
                                               "invariant to the bias value"):
                                    dv.tick(e.tensor_reduce(mneg[:], aux_ps[0:1, 384:512],
                                                            axis=AX, op=ALU.min))
                            mn_done = dv.v

                            @block.tensor
                            def _(e, mn_done=mn_done):
                                e.wait_ge(dv.h, mn_done)
                                pe.tick(e.matmul(aux_ps[:, 128:129], smrow_sb[0:1, :],
                                                 mneg[:], start=True, stop=True))
                            bcm_done = pe.v

                            @block.vector
                            def _(e, bcm_done=bcm_done):
                                e.wait_ge(pe.h, bcm_done)
                                dv.tick(e.tensor_copy(bias_sb[:], aux_ps[:, 128:129]))
                            bias_done = dv.v

                            @block.scalar
                            def _(e, P=P, uc=uc, bias_done=bias_done):
                                e.wait_ge(dv.h, bias_done)
                                e.wait_ge(dv.h, pv_free[uc - 2])
                                ac.tick(e.activation(probsT[:], sc_ps[P][:, 0:WIN * 128],
                                                     AF.Exp, bias=bias_sb[:], scale=SM))
                            exp_of[uc] = ac.v

                            @block.tensor
                            def _(e, t=t, kvh=kvh, uc=uc):
                                e.wait_ge(ac.h, exp_of[uc])
                                e.wait_ge(ac.h, pv_evac[uc - 2])
                                e.wait_ge(dv.h, pv_free[uc - 2])
                                for j in range(WIN):
                                    kt = t + j
                                    pe.tick(e.matmul(
                                        aux_ps[:, 0:128],
                                        V_sb[:, kt * KVH * HD + kvh * HD:
                                             kt * KVH * HD + (kvh + 1) * HD],
                                        probsT[:, j * 128:(j + 1) * 128],
                                        start=(j == 0), stop=(j == WIN - 1)))
                                    pe.tick(e.matmul(
                                        sc_ps[P][0:1, WIN * 128:WIN * 128 + 128],
                                        kv01_sb[:, kt:kt + 1],
                                        probsT[:, j * 128:(j + 1) * 128],
                                        start=(j == 0), stop=(j == WIN - 1)))
                            pv_done = pe.v

                            @block.vector
                            def _(e, pv_done=pv_done):
                                e.wait_ge(pe.h, pv_done)
                                with nc.allow_low_precision(
                                        reason="softmax normalizer applied as a bf16 "
                                               "matmul-operand-class factor"):
                                    dv.tick(e.reciprocal(rsum[:],
                                                         sc_ps[P][0:1, WIN * 128:WIN * 128 + 128]))
                            rs_done = dv.v
                            slot_free[uc] = (dv, rs_done)

                            @block.tensor
                            def _(e, rs_done=rs_done):
                                e.wait_ge(dv.h, rs_done)
                                pe.tick(e.matmul(aux_ps[:, 128:256], onesb_sb[0:1, :],
                                                 rsum[:], start=True, stop=True))
                            rb_done = pe.v

                            @block.scalar
                            def _(e, uc=uc, rb_done=rb_done):
                                e.wait_ge(pe.h, rb_done)
                                ac.tick(e.activation(attnU[:, (uc % 2) * 128:(uc % 2 + 1) * 128],
                                                     aux_ps[:, 0:128], AF.Copy, bias=0.0))
                            pv_evac[uc] = ac.v

                            @block.vector
                            def _(e, h=h, t=t, uc=uc, rb_done=rb_done):
                                e.wait_ge(pe.h, rb_done)
                                e.wait_ge(ac.h, pv_evac[uc])
                                if t == 0:
                                    s, v = attnH_st[h - 2]
                                    e.wait_ge(s.h, v)
                                dv.tick(e.tensor_copy(rb_sb[:], aux_ps[:, 128:256]))
                                e.wait_ge(dv.h, dv.v)
                                dv.tick(e.tensor_tensor(
                                    attnH[:, (h % 2) * TOK + t * 128:(h % 2) * TOK + (t + 1) * 128],
                                    attnU[:, (uc % 2) * 128:(uc % 2 + 1) * 128],
                                    rb_sb[:], ALU.mult))
                            pv_free[uc] = dv.v
                            if dump == "att0" and h == 0 and t == 0:
                                @block.gpsimd
                                def _(e):
                                    e.wait_ge(dv.h, pv_free[uc])
                                    qg.tick(e.dma_start(out=att0tap["q"], in_=Qh[:, 0:TOK]), 16)
                                    qg.tick(e.dma_start(out=att0tap["pr"], in_=probsT[:]), 16)
                                    e.wait_ge(qg.h, qg.v)
                                return nc
                            uc += 1

                        @block.scalar
                        def _(e, h=h):
                            e.wait_ge(dv.h, pv_free[uc - 1])
                            qa.tick(e.dma_start(out=attnD.ap()[h],
                                                in_=attnH[:, (h % 2) * TOK:(h % 2 + 1) * TOK]), 16)
                        attnH_st[h] = (qa, qa.v)

                    attn_all_qa = qa.v

        if dump == "attn":
            @block.gpsimd
            def _(e):
                e.wait_ge(qa.h, attn_all_qa)
                e.wait_ge(qg.h, qg.v)
            return nc

        with contextlib.ExitStack() as _S:
            attnT6 = _S.enter_context(nc.sbuf_tensor("attnT6", [128, H * TOK], BF16))
            wo_sb = _S.enter_context(nc.sbuf_tensor("wo_sb", [128, DMC * 1024], BF16))
            gaB = _S.enter_context(nc.sbuf_tensor("gaB", [128, DM], F32))
            rowbf2 = _S.enter_context(nc.sbuf_tensor("rowbf2", [1, DM], BF16))
            xg_sb = _S.enter_context(nc.sbuf_tensor("xg_sb", [128, 2 * 2048], F32))
            x2_sb = _S.enter_context(nc.sbuf_tensor("x2_sb", [128, 2 * 2048], F32))
            bc2_ps = _S.enter_context(nc.psum_tensor("bc2_ps", [128, 1024], F32))
            yo_ps0 = _S.enter_context(nc.psum_tensor("yo_ps0", [128, 1024], F32))
            yo_ps1 = _S.enter_context(nc.psum_tensor("yo_ps1", [128, 1024], F32))
            gaB_done = emit_bcast(2 * DM, gaB, rowbf2, bc2_ps,
                                  extra_wait=(qa, attn_all_qa))
            yo_ps = [yo_ps0, yo_ps1]

            @block.sync
            def _(e):
                e.wait_ge(qa.h, attn_all_qa)
                for h in range(H):
                    qs.tick(e.dma_start(out=attnT6[:, h * TOK:(h + 1) * TOK],
                                        in_=attnD.ap()[h]), 16)
            attn6_ld = qs.v

            wo_ld = {}
            wo_mm = {}
            x_ld2 = {}
            x2_dv = {}
            x2_st = {}

            def emit_x_load2(i, wait=None):
                g, tp = divmod(i, 4)

                @block.scalar
                def _(e):
                    if wait is not None:
                        e.wait_ge(dv.h, wait)
                    qa.tick(e.dma_start(
                        out=xg_sb[:, (i % 2) * 2048:(i % 2 + 1) * 2048]
                        .rearrange("p (a c) -> p a c", c=1024),
                        in_=x_own[tp * 256:(tp + 1) * 256, g * 1024:(g + 1) * 1024]
                        .rearrange("(a p) c -> p a c", p=128)), 16)
                x_ld2[i] = qs.v if False else qa.v

            emit_x_load2(0)
            emit_x_load2(1)

            for g in range(4):
                @block.gpsimd
                def _(e, g=g):
                    if g == 0:
                        e.wait_ge(qa.h, attn_all_qa)
                    if g >= 1:
                        e.wait_ge(pe.h, wo_mm[g * 4 - 1])
                    for kc in range(DMC):
                        qg.tick(e.dma_start(out=wo_sb[:, kc * 1024:(kc + 1) * 1024],
                                            in_=wo_g[g, kc]), 16)
                        wo_ld[(g, kc)] = qg.v

                for tp in range(4):
                    i = g * 4 + tp
                    for half in range(2):
                        t = tp * 2 + half

                        @block.tensor
                        def _(e, g=g, t=t, half=half, i=i):
                            if i == 0 and half == 0:
                                e.wait_ge(qs.h, attn6_ld)
                                e.wait_ge(ac.h, gaB_done)
                            if i >= 1 and half == 0:
                                e.wait_ge(dv.h, x2_dv[i - 1])
                            if i % 4 == 0 and half == 0:
                                e.wait_ge(qg.h, wo_ld[(g, DMC - 1)])
                            for hf in range(2):
                                for kc in range(DMC):
                                    pe.tick(e.matmul(
                                        yo_ps[half][:, hf * 512:(hf + 1) * 512],
                                        attnT6[:, kc * TOK + t * 128:kc * TOK + (t + 1) * 128],
                                        wo_sb[:, kc * 1024 + hf * 512:kc * 1024 + (hf + 1) * 512],
                                        start=(kc == 0), stop=(kc == DMC - 1)))
                    wo_mm[i] = pe.v

                    @block.vector
                    def _(e, g=g, i=i):
                        e.wait_ge(pe.h, wo_mm[i])
                        e.wait_ge(qa.h, x_ld2[i])
                        if i >= 2:
                            e.wait_ge(qs.h, x2_st[i - 2])
                        for half in range(2):
                            sl = slice((i % 2) * 2048 + half * 1024,
                                       (i % 2) * 2048 + (half + 1) * 1024)
                            dv.tick(e.tensor_tensor(x2_sb[:, sl], yo_ps[half][:],
                                                    gaB[:, g * 1024:(g + 1) * 1024],
                                                    ALU.mult))
                            e.wait_ge(dv.h, dv.v)
                            dv.tick(e.tensor_tensor(
                                x2_sb[:, sl], x2_sb[:, sl],
                                xg_sb[:, (i % 2) * 2048 + half * 1024:
                                       (i % 2) * 2048 + (half + 1) * 1024],
                                ALU.add))
                    x2_dv[i] = dv.v

                    @block.sync
                    def _(e, g=g, tp=tp, i=i):
                        if i >= 2:
                            e.wait_ge(qs.h, qs.v)
                        e.wait_ge(dv.h, x2_dv[i])
                        qs.tick(e.dma_start(
                            out=y[tp * 256:(tp + 1) * 256, g * 1024:(g + 1) * 1024]
                            .rearrange("(a p) c -> p a c", p=128),
                            in_=x2_sb[:, (i % 2) * 2048:(i % 2 + 1) * 2048]
                            .rearrange("p (a c) -> p a c", c=1024)), 16)
                    x2_st[i] = qs.v

                    if i + 2 < 16:
                        emit_x_load2(i + 2, wait=x2_dv[i])
            x2_written = qs.v

        if dump == "x2":
            @block.gpsimd
            def _(e):
                e.wait_ge(qs.h, x2_written)
                e.wait_ge(qg.h, qg.v)
            return nc

        with nc.sbuf_tensor("xqT", [128, DMC * TOK], BF16) as xqT:
            with contextlib.ExitStack() as _S:
                cbB = _S.enter_context(nc.sbuf_tensor("cbB", [128, DM], F32))
                sbB = _S.enter_context(nc.sbuf_tensor("sbB", [128, DM], F32))
                rowbf3 = _S.enter_context(nc.sbuf_tensor("rowbf3", [1, DM], BF16))
                x2r = _S.enter_context(nc.sbuf_tensor("x2r", [128, 2 * DM], F32))
                u2 = _S.enter_context(nc.sbuf_tensor("u2", [128, DM], F32))
                xqtm = _S.enter_context(nc.sbuf_tensor("xqtm", [128, DM], BF16))
                st2 = _S.enter_context(nc.sbuf_tensor("st2", [128, 16], F32))
                gm = _S.enter_context(nc.sbuf_tensor("gm", [128, GRP], F32))
                gmi = _S.enter_context(nc.sbuf_tensor("gmi", [128, GRP], F32))
                bc3_ps = _S.enter_context(nc.psum_tensor("bc3_ps", [128, 1024], F32))
                tq_ps = _S.enter_context(nc.psum_tensor("tq_ps", [128, 2048], BF16))
                cbB_done = emit_bcast(4 * DM, cbB, rowbf3, bc3_ps,
                                      extra_wait=(qs, x2_written))
                sbB_done = emit_bcast(3 * DM, sbB, rowbf3, bc3_ps,
                                      extra_wait=(qs, x2_written))

                x2_ld = [None] * NT
                q_sq = [None] * NT
                q_xread = [None] * NT
                q_xq = [None] * NT
                xqt_mm = [None] * NT

                def emit_x2_load(t):
                    @block.sync
                    def _(e):
                        if t == 0:
                            e.wait_ge(qs.h, x2_written)
                        if t >= 2:
                            e.wait_ge(ac.h, q_sq[t - 2])
                            e.wait_ge(dv.h, q_xread[t - 2])
                        qs.tick(e.dma_start(out=x2r[:, (t % 2) * DM:(t % 2 + 1) * DM],
                                            in_=y[t * 128:(t + 1) * 128, :]), 16)
                    x2_ld[t] = qs.v

                emit_x2_load(0)
                emit_x2_load(1)
                for t in range(NT):
                    sl = slice((t % 2) * DM, (t % 2 + 1) * DM)

                    @block.scalar
                    def _(e, t=t, sl=sl):
                        e.wait_ge(qs.h, x2_ld[t])
                        if t >= 1:
                            e.wait_ge(dv.h, q_xq[t - 1])
                        ac.tick(e.activation(u2[:], x2r[:, sl], AF.Square,
                                             accum_out=st2[:, 1:2]))
                    q_sq[t] = ac.v

                    @block.vector
                    def _(e, t=t, sl=sl):
                        e.wait_ge(qs.h, x2_ld[t])
                        e.wait_ge(ac.h, q_sq[t])
                        dv.tick(e.tensor_reduce(st2[:, 0:1], x2r[:, sl], axis=AX, op=ALU.add))
                        e.wait_ge(dv.h, dv.v)
                        dv.tick(e.tensor_scalar(st2[:, 2:3], st2[:, 0:1], 1.0 / DM, None, ALU.mult))
                        dv.tick(e.tensor_scalar(st2[:, 4:5], st2[:, 1:2], 1.0 / DM, None, ALU.mult))
                        e.wait_ge(dv.h, dv.v)
                        dv.tick(e.tensor_tensor(st2[:, 3:4], st2[:, 2:3], st2[:, 2:3], ALU.mult))
                        e.wait_ge(dv.h, dv.v)
                        dv.tick(e.tensor_tensor(st2[:, 5:6], st2[:, 4:5], st2[:, 3:4], ALU.subtract))
                        e.wait_ge(dv.h, dv.v)
                        dv.tick(e.tensor_scalar(st2[:, 5:6], st2[:, 5:6], EPS, None, ALU.add))
                    q_var = dv.v

                    @block.scalar
                    def _(e, q_var=q_var):
                        e.wait_ge(dv.h, q_var)
                        e.activation(st2[:, 6:7], st2[:, 5:6], AF.Ln)
                        ac.tick(e.activation(st2[:, 7:8], st2[:, 6:7], AF.Exp, scale=-0.5))
                    q_rstd = ac.v

                    @block.vector
                    def _(e, t=t, sl=sl, q_rstd=q_rstd):
                        e.wait_ge(ac.h, q_rstd)
                        dv.tick(e.tensor_tensor(st2[:, 8:9], st2[:, 2:3], st2[:, 7:8], ALU.mult))
                        e.wait_ge(dv.h, dv.v)
                        dv.tick(e.tensor_scalar(st2[:, 9:10], st2[:, 8:9], -1.0, None, ALU.mult))
                        e.wait_ge(dv.h, dv.v)
                        dv.tick(e.tensor_scalar(u2[:], x2r[:, sl], st2[:, 7:8],
                                                st2[:, 9:10], ALU.mult, ALU.add))
                        q_xread[t] = dv.v
                        e.wait_ge(dv.h, dv.v)
                        if t == 0:
                            e.wait_ge(ac.h, max(cbB_done, sbB_done))
                        if t >= 1:
                            e.wait_ge(pe.h, xqt_mm[t - 1])
                        dv.tick(e.tensor_tensor(u2[:], u2[:], cbB[:], ALU.mult))
                        e.wait_ge(dv.h, dv.v)
                        dv.tick(e.tensor_tensor(u2[:], u2[:], sbB[:], ALU.add))
                        e.wait_ge(dv.h, dv.v)
                        dv.tick(e.tensor_reduce(gm[:].unsqueeze(2),
                                                u2[:].rearrange("p (g k) -> p g k", k=GRP),
                                                axis=AX, op=ALU.max,
                                                apply_absolute_value=True))
                        e.wait_ge(dv.h, dv.v)
                        dv.tick(e.tensor_scalar(gm[:], gm[:], EPS, None, ALU.max))
                        e.wait_ge(dv.h, dv.v)
                        dv.tick(e.reciprocal(gmi[:], gm[:]))
                        e.wait_ge(dv.h, dv.v)
                        dv.tick(e.scalar_tensor_tensor(
                            u2[:].rearrange("p (g k) -> p g k", k=GRP),
                            u2[:].rearrange("p (g k) -> p g k", k=GRP),
                            QMAX,
                            gmi[:].unsqueeze(2).broadcast_to([128, GRP, GRP]),
                            ALU.mult, ALU.mult))
                        e.wait_ge(dv.h, dv.v)
                        dv.tick(e.tensor_scalar(u2[:], u2[:], RND, None, ALU.add))
                        e.wait_ge(dv.h, dv.v)
                        dv.tick(e.tensor_scalar(u2[:], u2[:], -RND, None, ALU.add))
                        e.wait_ge(dv.h, dv.v)
                        dv.tick(e.scalar_tensor_tensor(
                            xqtm[:].rearrange("p (g k) -> p g k", k=GRP),
                            u2[:].rearrange("p (g k) -> p g k", k=GRP),
                            1.0 / QMAX,
                            gm[:].unsqueeze(2).broadcast_to([128, GRP, GRP]),
                            ALU.mult, ALU.mult))
                    q_xq[t] = dv.v

                    for grp16 in range(2):
                        @block.tensor
                        def _(e, t=t, grp16=grp16):
                            e.wait_ge(dv.h, q_xq[t])
                            e.wait_ge(ac.h, ac.v)
                            for k in range(16):
                                kc = grp16 * 16 + k
                                pe.tick(e.transpose(tq_ps[:, k * 128:(k + 1) * 128],
                                                    xqtm[:, kc * 128:(kc + 1) * 128],
                                                    identb_sb[:]))
                        xqt_mm[t] = pe.v

                        @block.scalar
                        def _(e, t=t, grp16=grp16):
                            e.wait_ge(pe.h, pe.v)
                            ac.tick(e.activation(
                                xqT[:, :].rearrange("p (kc tt) -> p kc tt", tt=TOK)
                                [:, grp16 * 16:(grp16 + 1) * 16, t * 128:(t + 1) * 128],
                                tq_ps[:].rearrange("p (k q) -> p k q", q=128),
                                AF.Copy, bias=0.0))
                    if t + 2 < NT:
                        emit_x2_load(t + 2)
                xq_all = ac.v

            if dump == "xq":
                @block.gpsimd
                def _(e):
                    e.wait_ge(ac.h, xq_all)
                    qg.tick(e.dma_start(out=dbg, in_=xqT[:]), 16)
                    e.wait_ge(qg.h, qg.v)
                return nc

            with contextlib.ExitStack() as _S:
                wf_sb = _S.enter_context(nc.sbuf_tensor("wf_sb", [128, DMC * 1024], BF16))
                absw = _S.enter_context(nc.sbuf_tensor("absw", [128, 2 * 1024], BF16))
                srow = _S.enter_context(nc.sbuf_tensor("srow", [1, 1024], F32))
                grow2 = _S.enter_context(nc.sbuf_tensor("grow2", [1, 1024], BF16))
                gbrow = _S.enter_context(nc.sbuf_tensor("gbrow", [1, 1024], F32))
                thr_row = _S.enter_context(nc.sbuf_tensor("thr_row", [1, 1024], BF16))
                thrB = _S.enter_context(nc.sbuf_tensor("thrB", [128, 1024], BF16))
                nthrB = _S.enter_context(nc.sbuf_tensor("nthrB", [128, 1024], BF16))
                gbsB = _S.enter_context(nc.sbuf_tensor("gbsB", [128, 1024], F32))
                g1 = _S.enter_context(nc.sbuf_tensor("g1", [128, 1024], BF16))
                g2 = _S.enter_context(nc.sbuf_tensor("g2", [128, 1024], BF16))
                fout = _S.enter_context(nc.sbuf_tensor("fout", [128, 2 * 1024], F32))
                s_ps = _S.enter_context(nc.psum_tensor("s_ps", [128, 1024], F32))
                bb_ps = _S.enter_context(nc.psum_tensor("bb_ps", [128, 1024], F32))
                ff_ps0 = _S.enter_context(nc.psum_tensor("ff_ps0", [128, 1024], F32))
                ff_ps1 = _S.enter_context(nc.psum_tensor("ff_ps1", [128, 1024], F32))
                ff_ps = [ff_ps0, ff_ps1]
                wf_ld = {}
                abs_done = {}
                smm_done = {}
                t_done = {}
                ff_mm = {}
                ff_dv = {}
                ff_st = {}
                srow_done = {}

                for g in range(4):
                    @block.gpsimd
                    def _(e, g=g):
                        if g == 0:
                            e.wait_ge(ac.h, xq_all)
                        if g >= 1:
                            e.wait_ge(pe.h, ff_mm[(g - 1, NT - 1)])
                        for kc in range(DMC):
                            qg.tick(e.dma_start(out=wf_sb[:, kc * 1024:(kc + 1) * 1024],
                                                in_=wfT_g[g, kc]), 16)
                            wf_ld[(g, kc)] = qg.v

                    for kc in range(DMC):
                        @block.scalar
                        def _(e, g=g, kc=kc):
                            e.wait_ge(qg.h, wf_ld[(g, kc)])
                            if kc >= 2:
                                e.wait_ge(pe.h, smm_done[(g, kc - 2)])
                            ac.tick(e.activation(absw[:, (kc % 2) * 1024:(kc % 2 + 1) * 1024],
                                                 wf_sb[:, kc * 1024:(kc + 1) * 1024], AF.Abs))
                        abs_done[(g, kc)] = ac.v

                        @block.tensor
                        def _(e, g=g, kc=kc):
                            e.wait_ge(ac.h, abs_done[(g, kc)])
                            if g > 0 and kc == 0:
                                e.wait_ge(dv.h, srow_done[g - 1])
                            for hf in range(2):
                                pe.tick(e.matmul(
                                    s_ps[0:1, hf * 512:(hf + 1) * 512], onesb_sb[:, 0:1],
                                    absw[:, (kc % 2) * 1024 + hf * 512:
                                         (kc % 2) * 1024 + (hf + 1) * 512],
                                    start=(kc == 0), stop=(kc == DMC - 1)))
                        smm_done[(g, kc)] = pe.v

                    @block.gpsimd
                    def _(e, g=g):
                        if g >= 1:
                            e.wait_ge(dv.h, srow_done[g - 1])
                        qg.tick(e.dma_start(
                            out=gbrow[:],
                            in_=gates_d.ap()[0:1, 5 * DM + g * 1024:5 * DM + (g + 1) * 1024]), 16)
                    gb_ld = qg.v

                    @block.vector
                    def _(e, g=g, gb_ld=gb_ld):
                        e.wait_ge(pe.h, smm_done[(g, DMC - 1)])
                        e.wait_ge(qg.h, gb_ld)
                        dv.tick(e.tensor_scalar(srow[:], s_ps[0:1, :], 1.0 / DM, EPS,
                                                ALU.mult, ALU.max))
                        e.wait_ge(dv.h, dv.v)
                        dv.tick(e.tensor_tensor(grow2[:], gbrow[:], srow[:], ALU.mult))
                        dv.tick(e.tensor_scalar(thr_row[:], srow[:], 0.5, None, ALU.mult))
                        e.wait_ge(dv.h, dv.v)
                    srow_done[g] = dv.v

                    @block.tensor
                    def _(e, g=g):
                        e.wait_ge(dv.h, srow_done[g])
                        for hf in range(2):
                            pe.tick(e.matmul(bb_ps[:, hf * 512:(hf + 1) * 512],
                                             onesb_sb[0:1, :],
                                             thr_row[0:1, hf * 512:(hf + 1) * 512],
                                             start=True, stop=True))
                    thr_mm = pe.v

                    @block.vector
                    def _(e, thr_mm=thr_mm):
                        e.wait_ge(pe.h, thr_mm)
                        dv.tick(e.tensor_copy(thrB[:], bb_ps[:]))
                        e.wait_ge(dv.h, dv.v)
                        dv.tick(e.tensor_scalar(nthrB[:], thrB[:], -1.0, None, ALU.mult))
                    thr_dv = dv.v

                    @block.tensor
                    def _(e, thr_dv=thr_dv):
                        e.wait_ge(dv.h, thr_dv)
                        for hf in range(2):
                            pe.tick(e.matmul(bb_ps[:, hf * 512:(hf + 1) * 512],
                                             onesb_sb[0:1, :],
                                             grow2[0:1, hf * 512:(hf + 1) * 512],
                                             start=True, stop=True))
                    gbs_mm = pe.v

                    @block.vector
                    def _(e, gbs_mm=gbs_mm):
                        e.wait_ge(pe.h, gbs_mm)
                        dv.tick(e.tensor_copy(gbsB[:], bb_ps[:]))
                    gbs_done = dv.v

                    for kc in range(DMC):
                        @block.vector
                        def _(e, g=g, kc=kc, gbs_done=gbs_done):
                            if kc == 0:
                                e.wait_ge(dv.h, gbs_done)
                            wsl = wf_sb[:, kc * 1024:(kc + 1) * 1024]
                            dv.tick(e.tensor_tensor(g1[:], wsl, thrB[:], ALU.is_gt))
                            dv.tick(e.tensor_tensor(g2[:], wsl, nthrB[:], ALU.is_lt))
                            e.wait_ge(dv.h, dv.v)
                            dv.tick(e.tensor_tensor(wsl, g1[:], g2[:], ALU.subtract))
                        t_done[(g, kc)] = dv.v

                    for t in range(NT):
                        @block.tensor
                        def _(e, g=g, t=t):
                            e.wait_ge(dv.h, t_done[(g, DMC - 1)])
                            if g == 0 and t == 0:
                                e.wait_ge(ac.h, xq_all)
                            if t >= 2:
                                e.wait_ge(dv.h, ff_dv[(g, t - 2)])
                            for hf in range(2):
                                for kc in range(DMC):
                                    pe.tick(e.matmul(
                                        ff_ps[t % 2][:, hf * 512:(hf + 1) * 512],
                                        xqT[:, kc * TOK + t * 128:kc * TOK + (t + 1) * 128],
                                        wf_sb[:, kc * 1024 + hf * 512:kc * 1024 + (hf + 1) * 512],
                                        start=(kc == 0), stop=(kc == DMC - 1)))
                        ff_mm[(g, t)] = pe.v

                        @block.vector
                        def _(e, g=g, t=t):
                            e.wait_ge(pe.h, ff_mm[(g, t)])
                            prev = (g, t - 2) if t >= 2 else ((g - 1, NT - 2 + t) if g >= 1 else None)
                            if prev is not None:
                                e.wait_ge(qg.h, ff_st[prev])
                            dv.tick(e.tensor_tensor(fout[:, (t % 2) * 1024:(t % 2 + 1) * 1024],
                                                    ff_ps[t % 2][:], gbsB[:], ALU.mult))
                        ff_dv[(g, t)] = dv.v

                        @block.gpsimd
                        def _(e, g=g, t=t):
                            e.wait_ge(dv.h, ff_dv[(g, t)])
                            if g == 0 and t == 0:
                                e.wait_ge(qs.h, x2_written)
                            qg.tick(e.dma_start(
                                out=y[t * 128:(t + 1) * 128, g * 1024:(g + 1) * 1024],
                                in_=fout[:, (t % 2) * 1024:(t % 2 + 1) * 1024],
                                accum_op=ALU.add), 16)
                        ff_st[(g, t)] = qg.v

                @block.gpsimd
                def _(e):
                    e.wait_ge(qg.h, qg.v)
                    e.wait_ge(qs.h, qs.v)
                    e.wait_ge(qa.h, qa.v)

    return nc


def _host_inputs(L0_attn_x, L0_attn_wq, L0_attn_wk, L0_attn_wv, L0_attn_wo,
                 L0_ffn_weight, L0_c, L0_wmod):
    import ml_dtypes
    cnt = np.ascontiguousarray
    wqkv = np.concatenate([np.asarray(L0_attn_wq), np.asarray(L0_attn_wk),
                           np.asarray(L0_attn_wv)], axis=1)
    wqkv_t = cnt(wqkv.reshape(DMC, 128, 48, 128).transpose(2, 1, 0, 3))
    wo_gt = cnt(np.asarray(L0_attn_wo).reshape(DMC, 128, 4, 1024).transpose(2, 0, 1, 3))
    wfT = np.asarray(L0_ffn_weight).T
    wfT_gt = cnt(wfT.reshape(DMC, 128, 4, 1024).transpose(2, 0, 1, 3))

    identf = np.eye(128, dtype=np.float32)
    identb = identf.astype(ml_dtypes.bfloat16)
    onesb = np.ones((128, 128), dtype=ml_dtypes.bfloat16)
    smrow = np.full((1, 128), SM, dtype=np.float32).astype(ml_dtypes.bfloat16)
    ii = np.arange(128)
    tri_lo = np.where(ii[:, None] > ii[None, :], 0.0, NEG).astype(np.float32)
    tri_hi = np.where(ii[:, None] <= ii[None, :], 0.0, NEG).astype(np.float32)
    inv_freq = (10000.0 ** (-(np.arange(64, dtype=np.float32)) / np.float32(64.0))).astype(np.float32)

    x = np.asarray(L0_attn_x)
    in_maps = []
    for c in range(NCORES):
        b, r = c // 2, c % 2
        partner = c ^ 1
        pos = (np.float32(r * TOK) + np.arange(TOK, dtype=np.float32)).astype(np.float32)
        ang = (pos[:, None] * inv_freq[None, :]).astype(np.float32)
        ang = ang.reshape(NT, 128, 64).transpose(1, 0, 2).reshape(128, NT * 64)
        kv01c = np.ones((128, NKT), dtype=np.float32)
        kvidxc = (partner * 128 + np.arange(128, dtype=np.int32)).reshape(128, 1)
        vmaskc = np.ones((128, 1), dtype=np.float32)
        if r == 0:
            kv01c[:, 0:NT] = 0.0
            kvidxc = (c * 128 + np.arange(128, dtype=np.int32)).reshape(128, 1)
            vmaskc[:] = 0.0
        in_maps.append({
            "x_own": cnt(x[b, r * TOK:(r + 1) * TOK, :]),
            "wqkv_t": wqkv_t,
            "wo_g": wo_gt,
            "wfT_g": wfT_gt,
            "wmod_sl": cnt(np.asarray(L0_wmod)[c * 512:(c + 1) * 512, :]),
            "cT_sl": cnt(np.asarray(L0_c)[:, c * 512:(c + 1) * 512].T),
            "cosv": np.cos(ang).astype(np.float32),
            "sinv": np.sin(ang).astype(np.float32),
            "tri_lo": tri_lo,
            "tri_hi": tri_hi,
            "kv01": kv01c,
            "identb": identb,
            "identf": identf,
            "onesb": onesb,
            "smrow": smrow,
            "kvidx": kvidxc,
            "vmask": vmaskc,
            "modidx": (b * 128 + np.arange(128, dtype=np.int32)).reshape(128, 1),
        })
    return in_maps


def kernel(**inputs):
    in_maps = _host_inputs(**inputs)
    nc = build_program()
    res = run_bass_kernel_spmd(nc, in_maps, core_ids=list(range(NCORES)))
    out = np.empty((4, SEQ, DM), dtype=np.float32)
    for c in range(NCORES):
        b, r = c // 2, c % 2
        out[b, r * TOK:(r + 1) * TOK, :] = res.results[c]["y"]
    return out
